# version 14
# speedup vs baseline: 1.1407x; 1.0051x over previous
import numpy as np
import ml_dtypes
from contextlib import ExitStack
import concourse.bass as bass
import concourse.mybir as mybir

F32 = mybir.dt.float32
BF16 = mybir.dt.bfloat16
AF = mybir.ActivationFunctionType
ALU = mybir.AluOpType
BF = ml_dtypes.bfloat16

T = 2048
ST = 256
NST = T // ST
C = 64
NCH = ST // C
D = 1024
NFC = 8
NTOK = ST * NFC
KSEL = 2048
NBIS = 22


def host_prep(ins):
    x = np.ascontiguousarray(ins['L0_attn_x'])
    shards = [np.ascontiguousarray(x[b, h * T:(h + 1) * T, :].T)
              for b in range(4) for h in range(2)]
    w = {}
    w['KWT'] = np.ascontiguousarray(ins['L0_attn_key_w'].T)
    w['VWT'] = np.ascontiguousarray(ins['L0_attn_vq_w'][:1024].T)
    w['QWT'] = np.ascontiguousarray(ins['L0_attn_vq_w'][1024:].T)
    w['OWT'] = np.ascontiguousarray(ins['L0_attn_out_w'].T)
    w['KB'] = np.ascontiguousarray(ins['L0_attn_key_b'].reshape(NFC, 128).T)
    w['OB'] = np.ascontiguousarray(ins['L0_attn_out_b'].reshape(NFC, 128).T)
    w['GW'] = np.ascontiguousarray(ins['L0_ffn_w_gate_w'].reshape(NFC, 128, 8))
    w['GB'] = np.ascontiguousarray(ins['L0_ffn_w_gate_b'].reshape(1, 8))
    w['W1'] = np.ascontiguousarray(ins['L0_ffn_W1'])
    w['B1L'] = np.ascontiguousarray(
        ins['L0_ffn_b1'].reshape(8, 2, 128).transpose(2, 0, 1).reshape(128, 16))
    w['W2'] = np.ascontiguousarray(ins['L0_ffn_W2'])
    w['B2'] = np.ascontiguousarray(ins['L0_ffn_b2'])
    return shards, w


def host_consts():
    c = {}
    c['IDENT'] = np.eye(128, dtype=np.float32).astype(BF)
    c['IDF32'] = np.eye(128, dtype=np.float32)
    c['ONESCOL'] = np.ones((128, 1), np.float32).astype(BF)
    c['ONESROW'] = np.ones((1, 128), np.float32)
    c['ONESROWB'] = np.ones((1, 128), np.float32).astype(BF)
    c['NEGBIG'] = np.full((1, 128), -30000.0, np.float32).astype(BF)
    pat16 = np.ones((128, ST), np.float32); pat16[:, ::16] = 0.0
    pat64 = np.ones((128, ST), np.float32); pat64[:, ::C] = 0.0
    c['PAT16'] = pat16
    c['PAT64'] = pat64
    c['ONESST'] = np.ones((128, ST), np.float32)
    m = np.zeros((128, 64), np.float32)
    for tb in range(4):
        r = slice(tb * 32, tb * 32 + 16)
        m[r, tb * 16:tb * 16 + 16] = np.triu(np.ones((16, 16), np.float32))
        if tb in (0, 2):
            m[r, (tb + 1) * 16:(tb + 2) * 16] = 1.0
        if tb in (0, 1):
            m[r, 32:64] = 1.0
    c['EVMA'] = m.astype(BF)
    agg = np.zeros((128, 128), np.float32)
    for p in range(128):
        for q in range(128):
            if (p % 8) == (q % 8):
                agg[p, q] = 1.0
    c['AGG'] = agg
    e8 = np.zeros((128, 8), np.float32)
    for e in range(8):
        e8[e, e] = 1.0
    c['E8'] = e8
    L = np.zeros((128, 128), np.float32)
    for p in range(128):
        for q in range(128):
            if (p % 8) == (q % 8) and (p // 8) < (q // 8):
                L[p, q] = 1.0
    c['SBLT'] = L
    return c


CSHAPES = {'IDENT': ([128, 128], BF16), 'IDF32': ([128, 128], F32),
           'ONESCOL': ([128, 1], BF16),
           'ONESROW': ([1, 128], F32), 'ONESROWB': ([1, 128], BF16),
           'NEGBIG': ([1, 128], BF16),
           'PAT16': ([128, ST], F32), 'PAT64': ([128, ST], F32),
           'ONESST': ([128, ST], F32),
           'EVMA': ([128, 64], BF16),
           'AGG': ([128, 128], F32), 'E8': ([128, 8], F32),
           'SBLT': ([128, 128], F32)}


class Prog:
    def __init__(self, nc, ctx, semnames):
        self.nc = nc
        self.ops = {e: [] for e in ('t', 'v', 's', 'y', 'g')}
        self.sems = {}
        self.cnt = {}
        for s in semnames:
            self.sems[s] = ctx.enter_context(nc.semaphore(s))
            self.cnt[s] = 0

    def wait(self, eng, sem, val=None):
        v = self.cnt[sem] if val is None else val
        self.ops[eng].append(('wait', sem, v))

    def op(self, eng, fn, sem=None, n=None):
        if sem is None:
            sem = {'t': 't', 'v': 'v', 's': 's', 'y': 'dI', 'g': 'dG'}[eng]
        if n is None:
            n = 16 if eng in ('y', 'g') else 1
        if eng in ('v', 's'):
            self.ops[eng].append(('wait', eng, self.cnt[eng]))
        self.ops[eng].append(('opinc', fn, sem, n))
        self.cnt[sem] += n
        return self.cnt[sem]

    def run(self, block):
        sems = self.sems

        def runner(eng_ops):
            def f(eng):
                for item in eng_ops:
                    if item[0] == 'wait':
                        eng.wait_ge(sems[item[1]], item[2])
                    else:
                        item[1](eng).then_inc(sems[item[2]], item[3])
            return f
        block.tensor(runner(self.ops['t']))
        block.vector(runner(self.ops['v']))
        block.scalar(runner(self.ops['s']))
        block.sync(runner(self.ops['y']))
        block.gpsimd(runner(self.ops['g']))


def build(debug_names=(), phase_max=9):
    pm = phase_max
    nc = bass.Bass("TRN2", target_bir_lowering=False)
    ctx = ExitStack()
    B = {}

    dx = nc.dram_tensor("x", [D, T], F32, kind="ExternalInput").ap()
    dw = {}
    for n, shp in [('KWT', [D, D]), ('VWT', [D, D]), ('QWT', [D, D]), ('OWT', [D, D]),
                   ('KB', [128, 8]), ('OB', [128, 8]), ('GW', [NFC, 128, 8]),
                   ('GB', [1, 8]), ('W1', [8, D, 256]), ('B1L', [128, 16]),
                   ('W2', [8, 256, D]), ('B2', [8, D])]:
        dw[n] = nc.dram_tensor(n, shp, F32, kind="ExternalInput").ap()
    dc = {n: nc.dram_tensor(n, CSHAPES[n][0], CSHAPES[n][1], kind="ExternalInput").ap()
          for n in CSHAPES}
    drole = nc.dram_tensor("ROLE", [128, 1], F32, kind="ExternalInput").ap()
    dbsel = nc.dram_tensor("BSELP", [128, 4], F32, kind="ExternalInput").ap()
    dout = nc.dram_tensor("out", [D, T], F32, kind="ExternalOutput").ap()

    dqs = nc.dram_tensor("qs_scr", [128, NFC * T], BF16)
    dx1 = nc.dram_tensor("x1_scr", [D, T], F32)
    dstate = nc.dram_tensor("state_snd", [128, 512], F32)
    dstate_g = nc.dram_tensor("state_gth", [256, 512], F32)
    dmask = nc.dram_tensor("mask_snd", [8, T], F32)
    dmask_g = nc.dram_tensor("mask_gth", [64, T], F32)
    dsel = nc.dram_tensor("sel_scr", [128, 256], BF16)
    dnsel = nc.dram_tensor("nsel_scr", [128, 256], BF16)

    def sb(n, p, f, dt=F32, c=ctx):
        t_ = c.enter_context(nc.sbuf_tensor(n, [p, f], dt))
        B[n] = t_
        return t_

    wKQ = sb('wKQ', 128, 16384, BF16)
    wV = sb('wV', 128, 8192, BF16)
    stage = sb('stage', 128, 512, F32)
    cst = {}
    for n in CSHAPES:
        cst[n] = ctx.enter_context(nc.sbuf_tensor('k' + n, CSHAPES[n][0], CSHAPES[n][1]))
    kb = sb('kb', 128, 8)
    nkb = sb('nkb', 128, 8)
    ob = sb('ob', 128, 8)
    b1l = sb('b1l', 128, 16)
    role = sb('role', 128, 1)
    nrole = sb('nrole', 128, 1)
    bselp = sb('bselp', 128, 4)
    xs = sb('xs', 128, NTOK)
    rows = sb('rows', 1, 4 * ST)
    bro = sb('bro', 128, 2 * ST)
    xn1 = sb('xn1', 128, NTOK, BF16)
    sarg = sb('sarg', 128, NTOK)
    sexpT = sb('sexpT', 128, NTOK, BF16)
    hfm = sb('hfm', 128, T * NFC, BF16)
    Ssb = sb('Ssb', 128, 512)
    Sbf = sb('Sbf', 128, 512, BF16)
    carry = sb('carry', 128, 8)
    aCall = sb('aCall', 128, 256)
    gwb = sb('gwb', 128, 64, BF16)
    gbb = sb('gbb', 1, 8, BF16)

    ictx = ExitStack()
    sfu = sb('sfu', 128, NTOK, BF16, ictx)
    st2 = sb('st2', 128, NTOK, F32, ictx)
    sL16 = sb('sL16', 128, NTOK, F32, ictx)
    sL64 = sb('sL64', 128, NTOK, F32, ictx)
    e16n = sb('e16n', 128, NTOK, BF16, ictx)
    e32n = sb('e32n', 128, NTOK, BF16, ictx)
    e64n = sb('e64n', 128, NTOK, BF16, ictx)
    eshn = sb('eshn', 128, NTOK, BF16, ictx)
    bq16 = sb('bq16', 128, NTOK, BF16, ictx)
    bu16r = sb('bu16r', 128, NTOK, BF16, ictx)
    bu16s = sb('bu16s', 128, NTOK, BF16, ictx)
    bq32 = sb('bq32', 128, NTOK, BF16, ictx)
    bu32 = sb('bu32', 128, NTOK, BF16, ictx)
    bq64 = sb('bq64', 128, NTOK, BF16, ictx)
    bu64 = sb('bu64', 128, NTOK, BF16, ictx)
    bqs = sb('bqs', 128, NTOK, BF16, ictx)
    vtm = sb('vtm', 128, 2048, BF16, ictx)
    vpad = sb('vpad', 128, 1024, BF16, ictx)
    utm = sb('utm', 128, 1024, BF16, ictx)
    psb = sb('psb', 128, 512, BF16, ictx)

    ps = lambda n, p, f, dt=F32: ctx.enter_context(nc.psum_tensor(n, [p, f], dt))
    pj = [ps('pj0', 128, 512), ps('pj1', 128, 512)]
    pA = ps('pA', 128, 512)
    pB = ps('pB', 128, 512)
    putm = ps('putm', 128, 1024, BF16)
    pU = ps('pU', 128, 512)
    ph = ps('ph', 128, 512)
    psp = ps('psp', 128, 512)

    P = Prog(nc, ctx, ['dI', 'dG', 't', 'v', 's', 'cc1', 'cc2'])

    def v3(t_):
        return t_.ap()[:, 0:NTOK].rearrange("p (fc t) -> p fc t", fc=NFC)

    def v4(t_):
        return t_.ap().rearrange("p (fc c w) -> p fc c w", fc=NFC, c=NCH)

    def vb16(t_):
        return t_.ap().rearrange("p (fc b w) -> p fc b w", fc=NFC, b=ST // 16)

    def cend_view(t_, off, gran):
        n = ST // gran
        return t_.ap().rearrange("p (fc c w) -> p fc c w", fc=NFC, c=n)[
            :, :, :, off:off + 1].to_broadcast([128, NFC, n, gran])

    for dst, src in [(kb, dw['KB']), (ob, dw['OB']), (b1l, dw['B1L']), (role, drole),
                     (bselp, dbsel)]:
        P.op('y', (lambda e, dst=dst, src=src: e.dma_start(out=dst[:, :], in_=src)))
    for n in CSHAPES:
        P.op('y', (lambda e, n=n: e.dma_start(out=cst[n][:, :], in_=dc[n])))
    P.wait('v', 'dI')
    P.op('v', lambda e: e.tensor_scalar_mul(nkb[:, :], kb[:, :], -1.0))
    P.op('v', lambda e: e.tensor_scalar(out=nrole[:, :], in0=role[:, :], scalar1=-1.0, scalar2=1.0, op0=ALU.mult, op1=ALU.add))

    def conv_weight(dst, dst_col0, src_ap, nkc):
        for kc in range(nkc):
            for hv in range(2):
                P.wait('y', 'v')
                P.op('y', (lambda e, kc=kc, hv=hv: e.dma_start(
                    out=stage[:, :],
                    in_=src_ap[kc * 128:(kc + 1) * 128, hv * 512:(hv + 1) * 512])))
                P.wait('v', 'dI')
                P.op('v', (lambda e, kc=kc, hv=hv: e.tensor_copy(
                    out=dst[:, dst_col0 + kc * 1024 + hv * 512:dst_col0 + kc * 1024 + (hv + 1) * 512],
                    in_=stage[:, :])))

    conv_weight(wKQ, 0, dw['KWT'], 8)
    conv_weight(wV, 0, dw['VWT'], 8)
    conv_weight(wKQ, 8192, dw['QWT'], 8)
    P.op('v', lambda e: e.memset(pA[:, :], 0.0))
    P.op('v', lambda e: e.memset(pB[:, :], 0.0))
    P.op('v', lambda e: e.memset(vpad[:, :], 0.0))
    P.op('v', lambda e: e.memset(Ssb[:, :], 0.0))
    P.op('v', lambda e: e.memset(Sbf[:, :], 0.0))
    P.op('v', lambda e: e.memset(carry[:, :], 0.0))
    kq_v = P.cnt['v']

    R = lambda i: rows[0:1, (i - 1) * ST:i * ST]
    LN_SQ = [0]

    def emit_ln(src_buf, dst_slice_fn):
        P.op('s', (lambda e: e.activation(out=sexpT[:, :], in_=src_buf[:, :],
                                          func=AF.Square)))
        sq_s = P.cnt['s']
        LN_SQ[0] = sq_s
        P.op('v', (lambda e: e.tensor_copy(out=xn1[:, :], in_=src_buf[:, :])))
        P.wait('t', 'v')
        P.wait('t', 's', sq_s)

        def ln_stats(eng):
            for fc in range(NFC):
                eng.matmul(psp[0:1, 0:ST], cst['ONESCOL'][:, :],
                           xn1[:, fc * ST:(fc + 1) * ST],
                           start=(fc == 0), stop=(fc == NFC - 1))
            for fc in range(NFC):
                r = eng.matmul(psp[0:1, ST:2 * ST], cst['ONESCOL'][:, :],
                               sexpT[:, fc * ST:(fc + 1) * ST],
                               start=(fc == 0), stop=(fc == NFC - 1))
            return r
        P.op('t', ln_stats)
        P.wait('v', 't')
        P.op('v', lambda e: e.tensor_scalar_mul(R(2), psp[0:1, 0:ST], 1.0 / D))
        P.op('v', lambda e: e.tensor_scalar_mul(R(3), psp[0:1, ST:2 * ST], 1.0 / D))
        P.op('v', lambda e: e.tensor_mul(R(1), R(2), R(2)))
        P.op('v', lambda e: e.tensor_sub(R(3), R(3), R(1)))
        P.wait('s', 'v')
        P.op('s', lambda e: e.activation(out=R(1), in_=R(3), func=AF.Ln,
                                         bias=np.float32(1e-5)))
        P.op('s', lambda e: e.activation(out=R(4), in_=R(1), func=AF.Exp,
                                         scale=np.float32(-0.5)))
        P.wait('v', 's')
        P.op('v', lambda e: e.scalar_tensor_tensor(
            out=R(3), in0=R(2), scalar=-1.0, in1=R(4), op0=ALU.mult, op1=ALU.mult))
        P.wait('t', 'v')

        def ln_bcast(eng):
            eng.matmul(psp[:, 0:ST], cst['ONESROW'][:, :], R(4), start=True, stop=True)
            return eng.matmul(psp[:, ST:2 * ST], cst['ONESROW'][:, :], R(3),
                              start=True, stop=True)
        P.op('t', ln_bcast)
        P.wait('v', 't')
        P.op('v', lambda e: e.tensor_copy(out=bro[:, :], in_=psp[:, 0:2 * ST]))

        def xn_mul(eng):
            b3 = bro.ap().rearrange("p (k t) -> p k t", k=2)
            return eng.tensor_tensor(out=v3(sarg), in0=v3(src_buf),
                                     in1=b3[:, 0:1, :].to_broadcast([128, NFC, ST]),
                                     op=ALU.mult)
        P.op('v', xn_mul)

        def xn_add(eng):
            b3 = bro.ap().rearrange("p (k t) -> p k t", k=2)
            ov, _ = dst_slice_fn()
            return eng.tensor_tensor(out=ov, in0=v3(sarg),
                                     in1=b3[:, 1:2, :].to_broadcast([128, NFC, ST]),
                                     op=ALU.add)
        return P.op('v', xn_add)

    qs_g = [0]
    hdone_t = [0]
    prev_pev = [0]
    prev_lnv = [0]
    sexpT_rd = [0]
    for st in range(NST):
        t0 = st * ST
        P.wait('y', 'v', prev_lnv[0] if st else kq_v)
        P.wait('y', 's', LN_SQ[0] if st else P.cnt['s'])
        P.op('y', (lambda e, t0=t0: e.dma_start(
            out=v3(xs), in_=dx.rearrange("(fc p) t -> p fc t", p=128)[:, :, t0:t0 + ST])))
        x_ld = P.cnt['dI']
        P.wait('s', 'dI', x_ld)
        P.wait('s', 'v', sexpT_rd[0])
        P.wait('v', 'dI', x_ld)
        xn_v = emit_ln(xs, lambda: (v3(xn1), None))
        prev_lnv[0] = xn_v

        bank_rd = [P.cnt['s'], P.cnt['s']]
        P.wait('t', 'v', max(xn_v, kq_v, prev_pev[0]))
        for fcp in range(4):
            bk = fcp % 2
            P.wait('t', 's', bank_rd[bk])

            def kmm(eng, fcp=fcp, bk=bk):
                for i in range(2):
                    fc = fcp * 2 + i
                    for kc in range(NFC):
                        r = eng.matmul(pj[bk][:, i * ST:(i + 1) * ST],
                                       wKQ[:, kc * 1024 + fc * 128:kc * 1024 + (fc + 1) * 128],
                                       xn1[:, kc * ST:(kc + 1) * ST],
                                       start=(kc == 0), stop=(kc == NFC - 1))
                return r
            P.op('t', kmm)
            P.wait('s', 't')

            def kev(eng, fcp=fcp, bk=bk):
                for i in range(2):
                    fc = fcp * 2 + i
                    r = eng.activation(out=sarg[:, fc * ST:(fc + 1) * ST],
                                       in_=pj[bk][:, i * ST:(i + 1) * ST],
                                       func=AF.Exp, scale=np.float32(-1.0),
                                       bias=nkb[:, fc:fc + 1])
                return r
            bank_rd[bk] = P.op('s', kev)

        vev = [0, 0]
        for g in range(4):
            mw, fw = g // 2, g % 2
            bk = g % 2
            P.wait('t', 's', bank_rd[bk])
            P.wait('t', 'v', vev[bk])

            def vmm(eng, mw=mw, fw=fw, bk=bk):
                for kc in range(NFC):
                    r = eng.matmul(pj[bk][:, 0:512],
                                   xn1[:, kc * ST + mw * 128:kc * ST + (mw + 1) * 128],
                                   wV[:, kc * 1024 + fw * 512:kc * 1024 + (fw + 1) * 512],
                                   start=(kc == 0), stop=(kc == NFC - 1))
                return r
            P.op('t', vmm)
            P.wait('v', 't')
            vev[bk] = P.op('v', (lambda e, mw=mw, fw=fw, bk=bk: e.tensor_copy(
                out=vtm[:, mw * 1024 + fw * 512:mw * 1024 + (fw + 1) * 512],
                in_=pj[bk][:, 0:512])))
        v_v = P.cnt['v']

        hu_prev = hdone_t[0]
        s1c = P.op('s', lambda e: e.activation(out=st2[:, :], in_=sarg[:, :],
                                               func=AF.Ln, bias=np.float32(1.0)))
        s2c = P.op('s', lambda e: e.activation(out=sexpT[:, :], in_=st2[:, :],
                                               func=AF.Exp, scale=np.float32(-1.0)))

        P.wait('v', 's', s2c)
        P.op('v', lambda e: e.tensor_mul(sfu[:, :], sarg[:, :], sexpT[:, :]))
        v1c = P.cnt['v']

        def scans(eng):
            for fc in range(NFC):
                eng.tensor_tensor_scan(
                    out=sL16[:, fc * ST:(fc + 1) * ST], data0=cst['PAT16'][:, :],
                    data1=st2[:, fc * ST:(fc + 1) * ST],
                    initial=0.0, op0=ALU.mult, op1=ALU.add)
            for fc in range(NFC):
                r = eng.tensor_tensor_scan(
                    out=sL64[:, fc * ST:(fc + 1) * ST], data0=cst['PAT64'][:, :],
                    data1=st2[:, fc * ST:(fc + 1) * ST],
                    initial=0.0, op0=ALU.mult, op1=ALU.add)
            return r
        v2c = P.op('v', scans)
        P.op('v', lambda e: e.tensor_scalar_min(sL16[:, :], sL16[:, :], 75.0))

        def mk_lsh(eng):
            for fc in range(NFC):
                r = eng.tensor_tensor_scan(
                    out=sarg[:, fc * ST:(fc + 1) * ST], data0=cst['ONESST'][:, :],
                    data1=st2[:, fc * ST:(fc + 1) * ST],
                    initial=carry[:, fc:fc + 1], op0=ALU.mult, op1=ALU.add)
            return r
        P.wait('v', 's', s1c)
        P.op('v', mk_lsh)
        v6c = P.op('v', lambda e: e.tensor_copy(
            out=carry.ap().rearrange("p (fc o) -> p fc o", o=1),
            in_=sarg.ap().rearrange("p (fc t) -> p fc t", fc=NFC)[:, :, ST - 1:ST]))

        P.wait('s', 'v', v2c)
        P.op('s', lambda e: e.activation(out=e64n[:, :], in_=sL64[:, :], func=AF.Exp,
                                         scale=np.float32(-1.0)))
        P.wait('s', 'v', v6c)
        s3c = P.op('s', lambda e: e.activation(out=eshn[:, :], in_=sarg[:, :],
                                               func=AF.Exp, scale=np.float32(-1.0)))

        def mk_arg16(eng):
            return eng.tensor_tensor(out=vb16(sarg), in0=vb16(sL16),
                                     in1=cend_view(sL16, 15, 16), op=ALU.subtract)
        P.wait('v', 's', s3c)
        v7c = P.op('v', mk_arg16)

        P.wait('s', 'v', v7c)
        P.wait('s', 't', hu_prev)
        s4c = P.op('s', lambda e: e.activation(out=bq64[:, :], in_=sarg[:, :],
                                               func=AF.Exp))
        s5c = P.op('s', lambda e: e.activation(out=sexpT[:, :], in_=sL16[:, :],
                                               func=AF.Exp))
        P.wait('v', 's', s4c)
        P.op('v', lambda e: e.tensor_mul(bu16s[:, :], sfu[:, :], bq64[:, :]))
        P.wait('v', 's', s5c)
        P.op('v', lambda e: e.tensor_mul(bu16r[:, :], sfu[:, :], sexpT[:, :]))
        v9c = P.cnt['v']
        s6c = P.op('s', lambda e: e.activation(out=e16n[:, :], in_=sL16[:, :],
                                               func=AF.Exp, scale=np.float32(-1.0)))

        def mk_arg32(eng):
            return eng.tensor_tensor(out=v4(st2), in0=v4(sL64),
                                     in1=cend_view(sL64, 31, C), op=ALU.subtract)
        P.op('v', mk_arg32)
        P.op('v', lambda e: e.tensor_scalar_min(st2[:, :], st2[:, :], 60.0))
        v12c = P.op('v', lambda e: e.tensor_scalar_max(st2[:, :], st2[:, :], -60.0))
        P.wait('s', 'v', v12c)
        s7c = P.op('s', lambda e: e.activation(out=bq64[:, :], in_=st2[:, :],
                                               func=AF.Exp))
        s8c = P.op('s', lambda e: e.activation(out=e32n[:, :], in_=st2[:, :],
                                               func=AF.Exp, scale=np.float32(-1.0)))
        P.wait('v', 's', s7c)
        P.op('v', lambda e: e.tensor_mul(bu32[:, :], sfu[:, :], bq64[:, :]))

        def mk_arg64(eng):
            return eng.tensor_tensor(out=v4(sarg), in0=v4(sL64),
                                     in1=cend_view(sL64, C - 1, C), op=ALU.subtract)
        v14c = P.op('v', mk_arg64)

        def aC_exp(eng, st=st):
            iv = sL64.ap().rearrange("p (fc c w) -> p fc c w", fc=NFC, c=NCH)[:, :, :, C - 1:C]
            ov = aCall.ap().rearrange("p (fc c o) -> p fc c o", fc=NFC, o=1)[
                :, :, st * NCH:(st + 1) * NCH, :]
            return eng.activation(out=ov, in_=iv, func=AF.Exp, scale=np.float32(-1.0))
        P.op('s', aC_exp)
        P.wait('s', 'v', v14c)
        s10c = P.op('s', lambda e: e.activation(out=sexpT[:, :], in_=sarg[:, :],
                                                func=AF.Exp))
        ops_s = s8c

        qbanks = [pj[0], pj[1], pA, pB]
        P.wait('t', 'v', max(prev_pev[0], v_v))

        def qmm_all(eng):
            for fcp in range(4):
                for i in range(2):
                    fc = fcp * 2 + i
                    for kc in range(NFC):
                        r = eng.matmul(
                            qbanks[fcp][:, i * ST:(i + 1) * ST],
                            wKQ[:, 8192 + kc * 1024 + fc * 128:8192 + kc * 1024 + (fc + 1) * 128],
                            xn1[:, kc * ST:(kc + 1) * ST],
                            start=(kc == 0), stop=(kc == NFC - 1))
            return r
        P.op('t', qmm_all)
        q_t = P.cnt['t']
        P.wait('v', 's', ops_s)
        P.wait('v', 'dG', qs_g[0])
        P.wait('v', 't', q_t)

        def qev(eng, fcp, dsts):
            c0 = fcp * 512
            for dst, ee in dsts:
                r = eng.scalar_tensor_tensor(
                    out=dst[:, c0:c0 + 512], in0=qbanks[fcp][:, 0:512], scalar=1.0,
                    in1=ee[:, c0:c0 + 512], op0=ALU.mult, op1=ALU.mult)
            return r
        for fcp in range(4):
            P.op('v', (lambda e, fcp=fcp: qev(e, fcp, [(bq16, e16n), (bq32, e32n)])))
        for fcp in (2, 3):
            P.op('v', (lambda e, fcp=fcp: qev(e, fcp, [(bq64, e64n), (bqs, eshn)])))
        qv_b = P.cnt['v']
        for fcp in (0, 1):
            P.op('v', (lambda e, fcp=fcp: qev(e, fcp, [(bq64, e64n), (bqs, eshn)])))
        q_v = P.cnt['v']
        P.wait('v', 's', s10c)
        P.op('v', lambda e: e.tensor_mul(bu64[:, :], sfu[:, :], sexpT[:, :]))
        sexpT_rd[0] = P.cnt['v']
        P.wait('g', 'v', q_v)
        qs_g[0] = P.op('g', (lambda e, t0=t0: e.dma_start(
            out=dqs.rearrange("p (fc t) -> p fc t", fc=NFC)[:, :, t0:t0 + ST],
            in_=v3(bqs))))

        pb_map = [(pA, pB), (pj[0], pj[1])]
        pev_hist = {-1: q_v}
        utm_hist = {-1: P.cnt['v']}
        sv_hist = {-1: P.cnt['v']}

        def pieces(eng, tc, hx, ptile):
            rb = hx * 64
            for j in range(NFC):
                cj = j * ST + tc
                fo = j * 64
                for tb in range(4):
                    eng.matmul(ptile[tb * 32:tb * 32 + 16, fo + tb * 16:fo + (tb + 1) * 16],
                               bu16r[rb:rb + 64, cj + tb * 16:cj + (tb + 1) * 16],
                               bq16[rb:rb + 64, cj + tb * 16:cj + (tb + 1) * 16],
                               start=True, stop=True, tile_position=(rb, tb * 32))
                for tb in (0, 2):
                    eng.matmul(ptile[tb * 32:tb * 32 + 16, fo + (tb + 1) * 16:fo + (tb + 2) * 16],
                               bu16s[rb:rb + 64, cj + tb * 16:cj + (tb + 1) * 16],
                               bq16[rb:rb + 64, cj + (tb + 1) * 16:cj + (tb + 2) * 16],
                               start=True, stop=True, tile_position=(rb, tb * 32))
                for tb in (0, 1):
                    r = eng.matmul(ptile[tb * 32:tb * 32 + 16, fo + 32:fo + 64],
                                   bu32[rb:rb + 64, cj + tb * 16:cj + (tb + 1) * 16],
                                   bq32[rb:rb + 64, cj + 32:cj + 64],
                                   start=True, stop=True, tile_position=(rb, tb * 32))
            return r

        P.wait('t', 'v', qv_b)
        P.op('t', (lambda e: pieces(e, 0, 0, pb_map[0][0])))
        P.op('t', (lambda e: pieces(e, 0, 1, pb_map[0][1])))
        pieces_hist = {0: P.cnt['t']}

        for cix in range(NCH):
            ch = st * NCH + cix
            tc = cix * C
            par = tc % 128
            vcb = (tc // 128) * 1024
            bkA, bkB = pb_map[cix % 2]
            P.wait('g', 'v', v_v)
            P.wait('g', 't', hdone_t[0])
            for tb in range(4):
                P.op('g', (lambda e, tb=tb, par=par, vcb=vcb: e.dma_start(
                    out=vpad[tb * 32:tb * 32 + 16, :],
                    in_=vtm[par + tb * 16:par + tb * 16 + 16, vcb:vcb + 1024])))
            vpad_g = P.cnt['dG']

            P.wait('t', 'v', utm_hist[cix - 1])

            def u_trans(eng, tc=tc, par=par):
                for fc in range(NFC):
                    r = eng.transpose(putm[par:par + 64, fc * 128:(fc + 1) * 128],
                                      bu64[:, fc * ST + tc:fc * ST + tc + C],
                                      cst['IDENT'][:, :], tile_position=(0, par))
                return r
            P.op('t', u_trans)
            P.wait('v', 't')
            utm_hist[cix] = P.op('v', (lambda e, par=par: e.tensor_copy(
                out=utm[par:par + 64, :], in_=putm[par:par + 64, :])))

            if cix + 1 < NCH:
                nbkA, nbkB = pb_map[(cix + 1) % 2]
                P.wait('t', 'v', pev_hist[cix - 1])
                P.op('t', (lambda e, tc2=tc + C, nbkA=nbkA: pieces(e, tc2, 0, nbkA)))
                P.op('t', (lambda e, tc2=tc + C, nbkB=nbkB: pieces(e, tc2, 1, nbkB)))
                pieces_hist[cix + 1] = P.cnt['t']

            mv = cst['EVMA'].ap().rearrange("p (o t) -> p o t", o=1).to_broadcast([128, NFC, 64])
            P.wait('v', 't', pieces_hist[cix])
            P.op('v', (lambda e, mv=mv, bkA=bkA: e.tensor_tensor(
                out=psb.ap().rearrange("p (j t) -> p j t", j=NFC),
                in0=bkA.ap().rearrange("p (j t) -> p j t", j=NFC), in1=mv, op=ALU.mult)))
            pevA_v = P.cnt['v']

            def h_chain(eng, tc, hx):
                rb = hx * 64
                for j in range(NFC):
                    a = eng.matmul(ph[rb:rb + 64, j * 64:(j + 1) * 64],
                                   Sbf[rb:rb + 64, j * 64:(j + 1) * 64],
                                   bq64[rb:rb + 64, j * ST + tc:j * ST + tc + C],
                                   start=True, stop=False, tile_position=(rb, rb))
                    r = eng.matmul(ph[rb:rb + 64, j * 64:(j + 1) * 64],
                                   vpad[:, j * 128 + hx * 64:j * 128 + (hx + 1) * 64],
                                   psb[:, j * 64:(j + 1) * 64],
                                   start=False, stop=True, tile_position=(0, hx * 64))
                return r

            def U_mms(eng, tc, par, hx):
                for j in range(NFC):
                    r = eng.matmul(pU[hx * 64:(hx + 1) * 64, j * 64:(j + 1) * 64],
                                   utm[par:par + 64, j * 128 + hx * 64:j * 128 + (hx + 1) * 64],
                                   vtm[par:par + C,
                                       (tc // 128) * 1024 + j * 128 + hx * 64:
                                       (tc // 128) * 1024 + j * 128 + (hx + 1) * 64],
                                   start=True, stop=True, tile_position=(par, hx * 64))
                return r
            P.wait('t', 'v', max(pevA_v, sv_hist[cix - 1]))
            P.wait('t', 'dG', vpad_g)
            P.op('t', (lambda e, tc=tc: h_chain(e, tc, 0)))
            chainA_t = P.cnt['t']
            P.wait('t', 'v', utm_hist[cix])
            P.op('t', (lambda e, tc=tc, par=par: U_mms(e, tc, par, 0)))
            P.wait('v', 't', chainA_t)
            P.op('v', (lambda e, mv=mv, bkB=bkB: e.tensor_tensor(
                out=psb.ap().rearrange("p (j t) -> p j t", j=NFC),
                in0=bkB.ap().rearrange("p (j t) -> p j t", j=NFC), in1=mv, op=ALU.mult)))
            pevB_v = P.cnt['v']
            pev_hist[cix] = pevB_v
            P.wait('t', 'v', pevB_v)
            P.op('t', (lambda e, tc=tc: h_chain(e, tc, 1)))
            P.op('t', (lambda e, tc=tc, par=par: U_mms(e, tc, par, 1)))
            hU_t = P.cnt['t']
            hdone_t[0] = hU_t

            P.wait('v', 't', hU_t)
            P.op('v', (lambda e, st=st, tc=tc: e.tensor_copy(
                out=hfm.ap().rearrange("p (fc t) -> p fc t", fc=NFC)[
                    :, :, st * ST + tc:st * ST + tc + C],
                in_=ph.ap().rearrange("p (j t) -> p j t", j=NFC))))

            def s_scale(eng, ch=ch):
                acv = aCall.ap().rearrange("p (fc c) -> p fc c", fc=NFC)[
                    :, :, ch:ch + 1].to_broadcast([128, NFC, 64])
                return eng.tensor_tensor(out=Ssb.ap().rearrange("p (j e) -> p j e", j=NFC),
                                         in0=Ssb.ap().rearrange("p (j e) -> p j e", j=NFC),
                                         in1=acv, op=ALU.mult)
            P.op('v', s_scale)
            P.op('v', lambda e: e.tensor_tensor(out=Ssb[:, :], in0=Ssb[:, :],
                                                in1=pU[:, :], op=ALU.add))
            s_v = P.op('v', lambda e: e.tensor_copy(out=Sbf[:, :], in_=Ssb[:, :]))
            sv_hist[cix] = s_v
        prev_pev[0] = pev_hist[NCH - 1]

    class ConvPump:

        def __init__(self):
            self.rounds = []
            self.slot_rd = [0, 0]
            self.i = 0

        def add(self, dst, dst_col, src_view, width):
            self.rounds.append((dst, dst_col, src_view, width))

        def emit(self, n):
            for _ in range(n):
                if self.i >= len(self.rounds):
                    return
                dst, dcol, sview, wd = self.rounds[self.i]
                sl = self.i % 2
                off = sl * 256
                P.wait('y', 'v', self.slot_rd[sl])
                P.op('y', (lambda e, sview=sview, off=off, wd=wd: e.dma_start(
                    out=stage[:, off:off + wd], in_=sview)))
                ld = P.cnt['dI']
                P.wait('v', 'dI', ld)
                self.slot_rd[sl] = P.op('v', (lambda e, dst=dst, dcol=dcol, off=off, wd=wd:
                                              e.tensor_copy(out=dst[:, dcol:dcol + wd],
                                                            in_=stage[:, off:off + wd])))
                self.i += 1

        def done(self):
            return self.i >= len(self.rounds)

    pump = ConvPump()
    pumpW2 = ConvPump()
    pumpO = ConvPump()
    for e_ in range(8):
        for kc in range(8):
            pump.add(wKQ, e_ * 2048 + kc * 256,
                     dw['W1'][e_, kc * 128:(kc + 1) * 128, :], 256)
    for kc in range(8):
        for hv in range(4):
            pumpO.add(wV, kc * 1024 + hv * 256,
                      dw['OWT'][kc * 128:(kc + 1) * 128, hv * 256:(hv + 1) * 256], 256)

    if phase_max >= 2 or phase_max in (41, 42, 43):
        P.wait('g', 'v')
        P.op('g', lambda e: e.dma_start(out=dstate.ap(), in_=Ssb[:, :]))
        P.wait('g', 'dG')
        P.op('g', lambda e: e.collective_compute(
            "AllGather", ALU.bypass, replica_groups=[[0, 1], [2, 3], [4, 5], [6, 7]],
            ins=[dstate.ap().opt()], outs=[dstate_g.ap().opt()]), 'cc1', 1)
        P.wait('g', 'cc1', 1)
        P.op('g', lambda e: e.dma_start(out=stage[:, 0:512], in_=dstate_g[0:128, :]))
        P.wait('v', 'dG')
        P.op('v', lambda e: e.tensor_scalar(out=Sbf[:, :], in0=stage[:, 0:512],
                                            scalar1=role[:, 0:1], scalar2=None,
                                            op0=ALU.mult))
        sin_v = P.cnt['v']
        P.wait('v', 't')
        pumpO.slot_rd = [sin_v, sin_v]
        corr_rd = {0: P.cnt['v'], 1: P.cnt['v']}
        for j in range(NFC):
            P.wait('y', 'v', max(corr_rd.values()))
            P.op('y', (lambda e, j=j: e.dma_start(
                out=sexpT[:, 0:2048], in_=dqs[:, j * T:(j + 1) * T])))
            qld = P.cnt['dI']
            for w in range(4):
                pa_ = pA if w % 2 == 0 else pj[0]
                pb_ = pB if w % 2 == 0 else pj[1]
                P.wait('t', 'dI', qld)
                P.wait('t', 'v', max(sin_v, corr_rd[w % 2]))

                def corrmm(eng, j=j, w=w, pa_=pa_, pb_=pb_):
                    eng.matmul(pa_[0:64, 0:512],
                               Sbf[0:64, j * 64:(j + 1) * 64],
                               sexpT[0:64, w * 512:(w + 1) * 512],
                               start=True, stop=True, tile_position=(0, 0))
                    return eng.matmul(pb_[64:128, 0:512],
                                      Sbf[64:128, j * 64:(j + 1) * 64],
                                      sexpT[64:128, w * 512:(w + 1) * 512],
                                      start=True, stop=True, tile_position=(64, 64))
                P.op('t', corrmm)
                P.wait('v', 't')
                P.op('v', (lambda e, j=j, w=w, pa_=pa_: e.tensor_tensor(
                    out=hfm[0:64, j * T + w * 512:j * T + (w + 1) * 512],
                    in0=hfm[0:64, j * T + w * 512:j * T + (w + 1) * 512],
                    in1=pa_[0:64, 0:512], op=ALU.add)))
                corr_rd[w % 2] = P.op('v', (lambda e, j=j, w=w, pb_=pb_: e.tensor_tensor(
                    out=hfm[64:128, j * T + w * 512:j * T + (w + 1) * 512],
                    in0=hfm[64:128, j * T + w * 512:j * T + (w + 1) * 512],
                    in1=pb_[64:128, 0:512], op=ALU.add)))
            if j < 4:
                pumpO.emit(8)
            else:
                if j == 4:
                    pump.slot_rd = list(pumpO.slot_rd)
                pump.emit(16)



    xn2 = None
    x1b = None
    if phase_max >= 3 or phase_max in (41, 42, 43):
        ictx.close()
        xn2 = sb('xn2', 128, NFC * T, BF16)
        x1b = sb('x1b', 128, NTOK)
        for e_ in range(8):
            for kc2 in range(2):
                for hv in range(4):
                    pumpW2.add(hfm, e_ * 2048 + kc2 * 1024 + hv * 256,
                               dw['W2'][e_, kc2 * 128:(kc2 + 1) * 128, hv * 256:(hv + 1) * 256], 256)
        P.wait('y', 'v')
        P.op('y', lambda e: e.dma_start(
            out=stage.ap()[:, 0:64].rearrange("p (fc e) -> p fc e", fc=NFC),
            in_=dw['GW'].rearrange("fc p e -> p fc e")))
        P.op('y', lambda e: e.dma_start(out=stage[0:1, 64:72], in_=dw['GB']))
        P.wait('v', 'dI')
        P.op('v', lambda e: e.tensor_copy(out=gwb[:, :], in_=stage[:, 0:64]))
        P.op('v', lambda e: e.tensor_copy(out=gbb[:, :], in_=stage[0:1, 64:72]))
        ow_v = P.cnt['v']
        prev_x1v = [0]
        for st in range(NST):
            t0 = st * ST
            P.wait('y', 'v', prev_x1v[0] if st else P.cnt['v'])
            if st == 0:
                P.wait('y', 's', P.cnt['s'])
            P.op('y', (lambda e, t0=t0: e.dma_start(
                out=v3(xs),
                in_=dx.rearrange("(fc p) t -> p fc t", p=128)[:, :, t0:t0 + ST])))
            x_ld = P.cnt['dI']
            ob_rd = [P.cnt['v'], P.cnt['v']]
            for fcp in range(4):
                bk = fcp % 2
                P.wait('t', 'v', max(ob_rd[bk], ow_v))

                def omm(eng, fcp=fcp, bk=bk, t0=t0):
                    for i in range(2):
                        dc = fcp * 2 + i
                        for kc in range(NFC):
                            r = eng.matmul(
                                pj[bk][:, i * ST:(i + 1) * ST],
                                wV[:, kc * 1024 + dc * 128:kc * 1024 + (dc + 1) * 128],
                                hfm[:, kc * T + t0:kc * T + t0 + ST],
                                start=(kc == 0), stop=(kc == NFC - 1))
                    return r
                P.op('t', omm)
                P.wait('s', 't')
                P.wait('s', 'dI', x_ld)
                P.wait('s', 'v', ob_rd[bk])

                def oev(eng, fcp=fcp, bk=bk):
                    for i in range(2):
                        dc = fcp * 2 + i
                        r = eng.activation(out=x1b[:, dc * ST:(dc + 1) * ST],
                                           in_=pj[bk][:, i * ST:(i + 1) * ST],
                                           func=AF.Identity, bias=ob[:, dc:dc + 1])
                    return r
                sdone = P.op('s', oev)
                P.wait('v', 's', sdone)
                ob_rd[bk] = P.op('v', (lambda e, fcp=fcp: e.tensor_add(
                    x1b[:, fcp * 512:(fcp + 1) * 512], x1b[:, fcp * 512:(fcp + 1) * 512],
                    xs[:, fcp * 512:(fcp + 1) * 512])))
            x1_v = P.cnt['v']
            prev_x1v[0] = x1_v
            P.wait('g', 'v', x1_v)
            P.op('g', (lambda e, t0=t0: e.dma_start(
                out=dx1.rearrange("(fc p) t -> p fc t", p=128)[:, :, t0:t0 + ST],
                in_=v3(x1b))))
            P.wait('s', 'v', x1_v)
            emit_ln(x1b, (lambda t0=t0: (
                xn2.ap().rearrange("p (fc t) -> p fc t", fc=NFC)[:, :, t0:t0 + ST], None)))
            pump.emit(9)

    if phase_max >= 4 or phase_max in (41, 42, 43):
        mtm = sb('mtm', 128, 128)
        gz = sb('gz', 128, 16)
        gsm = sb('gsm', 128, 64)
        grow = sb('grow', 128, 8)
        mstg = sb('mstg', 8, 512)
        d4 = sb('d4', 128, 256)
        mmsl = sb('mmsl', 128, 256)
        bscr = sb('bscr', 128, 256)
        band = sb('band', 128, 256)
        bpre = sb('bpre', 128, 256)
        selw = sb('selw', 128, 256)
        th = sb('th', 128, 8)
        selbf = sb('selbf', 32, 2048, BF16)
        nselbf = sb('nselbf', 8, 2048, BF16)
        b2bf = sb('b2bf', 32, 1024, BF16)
        nsrow = sb('nsrow', 1, 2048, BF16)
        h_all = sb('h_all', 128, 4096, BF16)
        P.op('v', lambda e: e.memset(selbf[:, :], 0.0))
        P.op('v', lambda e: e.memset(b2bf[:, :], 0.0))
        jctx = ExitStack()
        G4 = sb('G4', 128, 1024, F32, jctx)
        selwb = sb('selwb', 128, 256, BF16, jctx)
        selbful = sb('selbful', 8, 4096, BF16, jctx)

        pump.emit(10 ** 6)
        P.wait('v', 't')
        pumpW2.emit(16)
        for hv in range(2):
            P.wait('y', 'v')
            P.op('y', (lambda e, hv=hv: e.dma_start(
                out=stage[0:8, :], in_=dw['B2'][:, hv * 512:(hv + 1) * 512])))
            P.wait('v', 'dI')
            P.op('v', (lambda e, hv=hv: e.tensor_copy(
                out=b2bf[0:8, hv * 512:(hv + 1) * 512], in_=stage[0:8, :])))
        wcv_v = P.cnt['v']

        g_rd = [P.cnt['v']]
        for w16 in range(16):
            tw = w16 * 128
            wq = w16 % 4
            if wq == 0:
                P.wait('t', 'v', g_rd[0])

            def gmm(eng, tw=tw, wq=wq):
                for kc in range(NFC):
                    eng.matmul(psp[:, wq * 8:wq * 8 + 8],
                               xn2[:, kc * T + tw:kc * T + tw + 128],
                               gwb[:, kc * 8:(kc + 1) * 8],
                               start=(kc == 0), stop=False)
                return eng.matmul(psp[:, wq * 8:wq * 8 + 8],
                                  cst['ONESROWB'][:, :], gbb[0:1, :],
                                  start=False, stop=True)
            P.op('t', gmm)
            if wq == 1:
                pumpW2.emit(8)
            if wq == 3:
                P.wait('v', 't')
                pspv = psp.ap()[:, 0:32].rearrange("p (w e) -> p w e", e=8)
                P.op('v', (lambda e, pspv=pspv: e.tensor_reduce(
                    out=gz.ap()[:, 0:4].rearrange("p (w o) -> p w o", o=1),
                    in_=pspv, axis=mybir.AxisListType.X, op=ALU.max)))
                P.op('v', (lambda e, pspv=pspv: e.tensor_tensor(
                    out=gsm.ap()[:, 0:32].rearrange("p (w e) -> p w e", e=8),
                    in0=pspv,
                    in1=gz.ap()[:, 0:4].rearrange("p (w o) -> p w o", o=1).to_broadcast([128, 4, 8]),
                    op=ALU.subtract)))
                P.wait('s', 'v')
                P.op('s', lambda e: e.activation(out=gsm[:, 32:64], in_=gsm[:, 0:32],
                                                 func=AF.Exp))
                P.wait('v', 's')
                P.op('v', lambda e: e.tensor_reduce(
                    out=gz.ap()[:, 4:8].rearrange("p (w o) -> p w o", o=1),
                    in_=gsm.ap()[:, 32:64].rearrange("p (w e) -> p w e", e=8),
                    axis=mybir.AxisListType.X, op=ALU.add))
                P.op('v', lambda e: e.reciprocal(out=gz[:, 8:12], in_=gz[:, 4:8]))
                P.op('v', lambda e: e.tensor_tensor(
                    out=gsm.ap()[:, 0:32].rearrange("p (w e) -> p w e", e=8),
                    in0=gsm.ap()[:, 32:64].rearrange("p (w e) -> p w e", e=8),
                    in1=gz.ap()[:, 8:12].rearrange("p (w o) -> p w o", o=1).to_broadcast([128, 4, 8]),
                    op=ALU.mult))
                for wq2 in range(4):
                    P.op('v', (lambda e: e.memset(grow[:, :], 0.0)))
                    for jx in range(8):
                        P.op('v', (lambda e, wq2=wq2, jx=jx: e.scalar_tensor_tensor(
                            out=grow[:, :], in0=gsm[:, wq2 * 8:(wq2 + 1) * 8],
                            scalar=gsm[:, wq2 * 8 + jx:wq2 * 8 + jx + 1],
                            in1=grow[:, :], op0=ALU.is_lt, op1=ALU.add)))
                    P.op('v', (lambda e: e.tensor_scalar(
                        out=grow[:, :], in0=grow[:, :], scalar1=2.0, scalar2=None,
                        op0=ALU.is_lt)))
                    g_rd[0] = P.op('v', (lambda e, w16=w16, wq2=wq2: e.tensor_mul(
                        mtm[:, (w16 - 3 + wq2) * 8:(w16 - 2 + wq2) * 8],
                        gsm[:, wq2 * 8:(wq2 + 1) * 8], grow[:, :])))

        pumpW2.emit(10 ** 6)
        mt_rd = [P.cnt['v'], P.cnt['dG']]
        for grp in range(4):
            P.wait('t', 'v', g_rd[0])
            P.wait('t', 'v', mt_rd[0])

            def mtr(eng, grp=grp):
                for i in range(4):
                    w16 = grp * 4 + i
                    r = eng.transpose(psp[0:8, i * 128:(i + 1) * 128],
                                      mtm[:, w16 * 8:(w16 + 1) * 8],
                                      cst['IDF32'][:, :])
                return r
            P.op('t', mtr)
            P.wait('v', 't')
            P.wait('v', 'dG', mt_rd[1])
            mt_rd[0] = P.op('v', (lambda e: e.tensor_copy(
                out=mstg[0:8, 0:512], in_=psp[0:8, 0:512])))
            P.wait('g', 'v', mt_rd[0])
            mt_rd[1] = P.op('g', (lambda e, grp=grp: e.dma_start(
                out=dmask[0:8, grp * 512:(grp + 1) * 512], in_=mstg[0:8, 0:512])))

        P.wait('g', 'dG')
        P.op('g', lambda e: e.collective_compute(
            "AllGather", ALU.bypass, replica_groups=[[0, 1, 2, 3, 4, 5, 6, 7]],
            ins=[dmask.ap().opt()], outs=[dmask_g.ap().opt()]), 'cc2', 1)
        P.wait('g', 'cc2', 1)
        for bp in range(4):
            for sbg in range(16):
                hl = sbg // 8
                P.op('g', (lambda e, bp=bp, sbg=sbg, hl=hl: e.dma_start(
                    out=G4[sbg * 8:(sbg + 1) * 8, bp * 256:(bp + 1) * 256],
                    in_=dmask_g[(2 * bp + hl) * 8:(2 * bp + hl) * 8 + 8,
                                (sbg % 8) * 256:(sbg % 8 + 1) * 256])))
        g4_g = P.cnt['dG']
        P.wait('v', 'dG', g4_g)
        P.op('v', lambda e: e.tensor_add(d4[:, :], G4[:, 0:256], G4[:, 256:512]))
        P.op('v', lambda e: e.tensor_add(d4[:, :], d4[:, :], G4[:, 512:768]))
        P.op('v', lambda e: e.tensor_add(d4[:, :], d4[:, :], G4[:, 768:1024]))
        P.op('v', lambda e: e.tensor_scalar_add(d4[:, :], d4[:, :], 1e-6))
        P.op('v', lambda e: e.memset(mmsl[:, :], 0.0))
        for bp in range(4):
            P.op('v', (lambda e, bp=bp: e.scalar_tensor_tensor(
                out=mmsl[:, :], in0=G4[:, bp * 256:(bp + 1) * 256],
                scalar=bselp[:, bp:bp + 1], in1=mmsl[:, :],
                op0=ALU.mult, op1=ALU.add)))

        P.op('v', lambda e: e.memset(th[:, 0:1], 0.0))
        P.op('v', lambda e: e.memset(th[:, 1:2], 1.0001))
        P.op('v', lambda e: e.memset(th[:, 2:3], 0.50005))
        for it in range(NBIS):
            P.op('v', (lambda e: e.scalar_tensor_tensor(
                out=bscr[:, :], in0=d4[:, :], scalar=th[:, 2:3], in1=mmsl[:, :],
                op0=ALU.mult, op1=ALU.is_lt, accum_out=th[:, 3:4])))
            P.wait('t', 'v')
            P.op('t', (lambda e, it=it: e.matmul(
                psp[:, (it % 2) * 4:(it % 2) * 4 + 1], cst['AGG'][:, :], th[:, 3:4],
                start=True, stop=True)))
            P.wait('v', 't')
            P.op('v', (lambda e, it=it: e.tensor_scalar(
                out=th[:, 4:5], in0=psp[:, (it % 2) * 4:(it % 2) * 4 + 1],
                scalar1=float(KSEL), scalar2=None, op0=ALU.is_ge)))
            P.op('v', lambda e: e.tensor_sub(th[:, 5:6], th[:, 2:3], th[:, 0:1]))
            P.op('v', lambda e: e.scalar_tensor_tensor(
                out=th[:, 0:1], in0=th[:, 5:6], scalar=th[:, 4:5], in1=th[:, 0:1],
                op0=ALU.mult, op1=ALU.add))
            P.op('v', lambda e: e.tensor_sub(th[:, 5:6], th[:, 1:2], th[:, 2:3]))
            P.op('v', lambda e: e.scalar_tensor_tensor(
                out=th[:, 1:2], in0=th[:, 5:6], scalar=th[:, 4:5], in1=th[:, 2:3],
                op0=ALU.mult, op1=ALU.add))
            P.op('v', lambda e: e.tensor_add(th[:, 2:3], th[:, 0:1], th[:, 1:2]))
            P.op('v', lambda e: e.tensor_scalar_mul(th[:, 2:3], th[:, 2:3], 0.5))

        P.op('v', lambda e: e.scalar_tensor_tensor(
            out=selw[:, :], in0=d4[:, :], scalar=th[:, 1:2], in1=mmsl[:, :],
            op0=ALU.mult, op1=ALU.is_lt, accum_out=th[:, 3:4]))
        P.wait('t', 'v')
        P.op('t', lambda e: e.matmul(psp[:, 8:9], cst['AGG'][:, :], th[:, 3:4],
                                     start=True, stop=True))
        P.wait('v', 't')
        P.op('v', lambda e: e.tensor_scalar(
            out=th[:, 5:6], in0=psp[:, 8:9], scalar1=-1.0, scalar2=float(KSEL),
            op0=ALU.mult, op1=ALU.add))
        P.op('v', lambda e: e.scalar_tensor_tensor(
            out=band[:, :], in0=d4[:, :], scalar=th[:, 0:1], in1=mmsl[:, :],
            op0=ALU.mult, op1=ALU.is_le))
        P.op('v', lambda e: e.tensor_sub(band[:, :], band[:, :], selw[:, :]))
        P.op('v', lambda e: e.reduce_sum(out=th[:, 6:7], in_=band[:, :],
                                         axis=mybir.AxisListType.X))
        P.wait('t', 'v')
        P.op('t', lambda e: e.matmul(psp[:, 9:10], cst['SBLT'][:, :], th[:, 6:7],
                                     start=True, stop=True))
        P.wait('v', 't')
        P.op('v', lambda e: e.tensor_copy(out=th[:, 7:8], in_=psp[:, 9:10]))
        P.op('v', lambda e: e.tensor_tensor_scan(
            out=bpre[:, :], data0=cst['ONESST'][:, 0:256], data1=band[:, :],
            initial=0.0, op0=ALU.mult, op1=ALU.add))
        P.op('v', lambda e: e.tensor_sub(bpre[:, :], bpre[:, :], band[:, :]))
        P.op('v', lambda e: e.tensor_scalar(
            out=bpre[:, :], in0=bpre[:, :], scalar1=th[:, 7:8], scalar2=None,
            op0=ALU.add))
        P.op('v', lambda e: e.tensor_scalar(
            out=bscr[:, :], in0=bpre[:, :], scalar1=th[:, 5:6], scalar2=None,
            op0=ALU.is_lt))
        P.op('v', lambda e: e.tensor_mul(bscr[:, :], bscr[:, :], band[:, :]))
        P.op('v', lambda e: e.tensor_add(selw[:, :], selw[:, :], bscr[:, :]))
        P.op('v', lambda e: e.tensor_copy(out=selwb[:, :], in_=selw[:, :]))
        sel_v = P.cnt['v']
        P.wait('g', 'v', sel_v)
        P.op('g', lambda e: e.dma_start(out=dsel.ap(), in_=selwb[:, :]))
        P.wait('g', 'dG')
        for e_ in range(8):
            P.op('g', (lambda e, e_=e_: e.dma_start(
                out=selbful.ap()[e_:e_ + 1, :].rearrange("o (sb t) -> o sb t", sb=16),
                in_=dsel.ap().rearrange("(sb ee) t -> ee sb t", ee=8)[e_:e_ + 1, :, :])))
        P.wait('v', 'dG')
        P.op('v', lambda e: e.tensor_scalar(
            out=selbf[0:8, :], in0=selbful[0:8, 0:2048], scalar1=nrole[0:8, 0:1],
            scalar2=None, op0=ALU.mult))
        P.op('v', lambda e: e.scalar_tensor_tensor(
            out=selbf[0:8, :], in0=selbful[0:8, 2048:4096], scalar=role[0:8, 0:1],
            in1=selbf[0:8, :], op0=ALU.mult, op1=ALU.add))
        P.op('v', lambda e: e.tensor_scalar(
            out=nselbf[0:8, :], in0=selbf[0:8, :], scalar1=-1.0, scalar2=1.0,
            op0=ALU.mult, op1=ALU.add))
        selbf_v = P.cnt['v']
        jctx.close()

    if phase_max >= 5 or phase_max in (41, 42, 43):
        out_g = [0]
        for st in range(NST):
            t0 = st * ST
            P.wait('y', 'v', P.cnt['v'])
            P.op('y', (lambda e, t0=t0: e.dma_start(
                out=v3(xs),
                in_=dx1.rearrange("(fc p) t -> p fc t", p=128)[:, :, t0:t0 + ST])))
            x1_ld = P.cnt['dI']
            P.wait('g', 'v', max(selbf_v, P.cnt['v'] if st == 0 else 0))
            P.wait('g', 't', P.cnt['t'])
            for e_ in range(8):
                P.op('g', (lambda e, e_=e_, t0=t0: e.dma_start(
                    out=nsrow[0:1, e_ * ST:(e_ + 1) * ST],
                    in_=nselbf[e_:e_ + 1, t0:t0 + ST])))
            nsr_g = P.cnt['dG']
            hall_rd = [P.cnt['s']]
            w1_rd = [P.cnt['v'], P.cnt['v']]
            for e_ in range(8):
                bkm = [0, 1]
                P.wait('t', 'dG', nsr_g)
                P.wait('t', 'dI', x1_ld)
                P.wait('t', 's', hall_rd[0])

                def w1mm(eng, e_=e_, t0=t0):
                    for mc in range(2):
                        for kc in range(NFC):
                            eng.matmul(
                                pj[mc][:, 0:ST],
                                wKQ[:, e_ * 2048 + kc * 256 + mc * 128:
                                    e_ * 2048 + kc * 256 + (mc + 1) * 128],
                                xn2[:, kc * T + t0:kc * T + t0 + ST],
                                start=(kc == 0), stop=False)
                        r = eng.matmul(pj[mc][:, 0:ST], cst['NEGBIG'][:, :],
                                       nsrow[0:1, e_ * ST:(e_ + 1) * ST],
                                       start=False, stop=True)
                    return r
                P.op('t', w1mm)
                P.wait('s', 't')

                def gel(eng, e_=e_):
                    for mc in range(2):
                        r = eng.activation(
                            out=h_all[:, (e_ * 2 + mc) * ST:(e_ * 2 + mc + 1) * ST],
                            in_=pj[mc][:, 0:ST], func=AF.Gelu,
                            bias=b1l[:, e_ * 2 + mc:e_ * 2 + mc + 1])
                    return r
                hall_rd[0] = P.op('s', gel)
            hall_s = P.cnt['s']
            ob_rd = [P.cnt['v'], P.cnt['v']]
            P.wait('t', 's', hall_s)
            for dcx in (range(NFC) if phase_max != 41 else []):
                bk = dcx % 2
                pb_ = [pA, pB][bk]
                P.wait('t', 'v', ob_rd[bk])

                def w2mm(eng, dcx=dcx, pb_=pb_, t0=t0):
                    first = True
                    for e_ in range(8):
                        for kc2 in range(2):
                            eng.matmul(
                                pb_[:, 0:ST],
                                hfm[:, e_ * 2048 + kc2 * 1024 + dcx * 128:
                                    e_ * 2048 + kc2 * 1024 + (dcx + 1) * 128],
                                h_all[:, (e_ * 2 + kc2) * ST:(e_ * 2 + kc2 + 1) * ST],
                                start=first, stop=False)
                            first = False
                    return eng.matmul(pb_[:, 0:32 * 0 + ST] if False else pb_[:, 0:ST],
                                      b2bf[0:32, dcx * 128:(dcx + 1) * 128],
                                      selbf[0:32, t0:t0 + ST], start=False, stop=True)
                P.op('t', w2mm)
                P.wait('v', 't')
                P.wait('v', 'dI', x1_ld)
                ob_rd[bk] = P.op('v', (lambda e, dcx=dcx, pb_=pb_: e.tensor_add(
                    x1b[:, dcx * ST:(dcx + 1) * ST], pb_[:, 0:ST],
                    xs[:, dcx * ST:(dcx + 1) * ST])))
            out_v = P.cnt['v']
            if phase_max != 41:
                P.wait('g', 'v', out_v)
                out_g[0] = P.op('g', (lambda e, t0=t0: e.dma_start(
                    out=dout.rearrange("(fc p) t -> p fc t", p=128)[:, :, t0:t0 + ST],
                    in_=v3(x1b))))

    _finish(P, nc, B, debug_names, dout, xs, dummy_out=(phase_max < 5 and phase_max not in (42,)))
    with nc.Block() as block:
        P.run(block)
    if phase_max < 3:
        ictx.close()
    ctx.close()
    return nc


def _finish(P, nc, B, debug_names, dout, xs, dummy_out=True):
    P.wait('g', 'v')
    P.wait('g', 's')
    P.wait('g', 't')
    for n in debug_names:
        src = B[n]
        shp = list(src.shape)
        dd = nc.dram_tensor("dbg_" + n, shp, src.dtype, kind="ExternalOutput").ap()
        P.op('g', (lambda e, dd=dd, src=src: e.dma_start(out=dd, in_=src[:, :])))
    if dummy_out:
        P.op('g', (lambda e: e.dma_start(
            out=dout.rearrange('(fc p) t -> p fc t', p=128)[:, :, 0:ST],
            in_=xs.ap().rearrange('p (fc t) -> p fc t', fc=NFC)[:, :, 0:ST])))
    P.wait('g', 'dG')



def kernel(**inputs):
    from concourse.bass_utils import run_bass_kernel_spmd
    shards, w = host_prep(inputs)
    consts = host_consts()
    nc = build(debug_names=(), phase_max=9)
    in_maps = []
    for cc in range(8):
        bsel = np.zeros((128, 4), np.float32)
        bsel[:, cc // 2] = 1.0
        m = {"x": shards[cc],
             "ROLE": np.full((128, 1), float(cc % 2), np.float32),
             "BSELP": bsel}
        m.update(w)
        m.update(consts)
        in_maps.append(m)
    res = run_bass_kernel_spmd(nc, in_maps, core_ids=list(range(8)))
    y = np.empty((4, 4096, 1024), np.float32)
    for cc in range(8):
        b, h = cc // 2, cc % 2
        y[b, h * T:(h + 1) * T, :] = np.asarray(res.results[cc]['out'], np.float32).T
    return y
